# version 05 (final)
# baseline (speedup 1.0000x reference)
import numpy as np
import ml_dtypes
from contextlib import ExitStack

import concourse.bass as bass
import concourse.mybir as mybir
from concourse.bass_utils import run_bass_kernel_spmd

F32 = mybir.dt.float32
BF16 = mybir.dt.bfloat16
U32 = mybir.dt.uint32
AF = mybir.ActivationFunctionType
ALU = mybir.AluOpType
bfloat16 = ml_dtypes.bfloat16

NCORES = 8
TB = 1024
TPC = 2 * TB
CD = 2048
KC = 16
QW = 64
NCHB = TB // QW
NEG = -1e30


class Buf:
    __slots__ = ("name", "writer", "readers")
    def __init__(self, name):
        self.name = name
        self.writer = None
        self.readers = {}

ENGSEM = {"pe": "pe", "dve": "dve", "act": "act", "pool": "pool", "sp": None}
DMASEM = {"sp": "spd", "pool": "poold", "act": "actd"}
ENG2BLK = {"pe": "tensor", "dve": "vector", "act": "scalar", "pool": "gpsimd", "sp": "sync"}


class Prog:
    def __init__(self, nc, stack, paranoid=False, arena_elems=65536):
        self.nc = nc
        self.stack = stack
        self.paranoid = paranoid
        self.arena_elems = arena_elems
        self.ops = {k: [] for k in ENG2BLK}
        self.cnt = {}
        self.sems = {}
        for s in ("pe", "dve", "act", "pool", "spd", "poold", "actd"):
            self.sems[s] = stack.enter_context(nc.semaphore(s))
            self.cnt[s] = 0
        self.waited = {k: {} for k in ENG2BLK}
        self._sb = {}
        self._ps = {}
        self._ab = {}
        self.n_instr = 0
        self.pseudo = {}
        self.dma_sems = {}

    def sbuf(self, key, shape, dtype):
        if key not in self._sb:
            h = self.stack.enter_context(self.nc.sbuf_tensor(key, list(shape), dtype))
            self._sb[key] = (h, Buf(key))
        return self._sb[key]

    def psum(self, key, shape, dtype=F32):
        if key not in self._ps:
            h = self.stack.enter_context(self.nc.psum_tensor(key, list(shape), dtype))
            self._ps[key] = (h, Buf(key))
        return self._ps[key]

    def _wait(self, eng, tok):
        if tok is None:
            return
        s, v, isdma = tok[0], tok[1], tok[2]
        pseudo = len(tok) > 3 and tok[3]
        if isdma:
            v = self.cnt[s]
        elif s == ENGSEM[eng]:
            if pseudo or v > self.cnt[s]:
                return
            if not self.paranoid and v < self.cnt[s]:
                return
        if v <= self.waited[eng].get(s, 0):
            return
        self.waited[eng][s] = v
        self.ops[eng].append(("w", s, v))

    def _pre(self, eng, reads, writes):
        for b in reads:
            if b is not None:
                self._wait(eng, b.writer)
        for b in writes:
            if b is None:
                continue
            self._wait(eng, b.writer)
            for s, (v, isdma) in b.readers.items():
                self._wait(eng, (s, v, isdma))

    def _post(self, tok, reads, writes):
        for b in writes:
            if b is not None:
                b.writer = tok
                b.readers = {}
        for b in reads:
            if b is not None and tok is not None:
                s, v, isdma = tok[0], tok[1], tok[2]
                old = b.readers.get(s)
                if old is None or old[0] < v:
                    b.readers[s] = (v, isdma)

    def raw(self, eng, build, reads=(), writes=(), inc=1):
        if getattr(self, "serial", False):
            self.fence()
        elif eng in getattr(self, "serial_engines", ()):
            self.fence_one(eng)
        self._pre(eng, reads, writes)
        sem = ENGSEM[eng]
        if sem is None:
            tok = None
            self.ops[eng].append(("op", build, 0, sem))
            self.n_instr += 1
            self._post(tok, reads, writes)
            return tok
        if inc:
            self.cnt[sem] += inc
            tok = (sem, self.cnt[sem], False)
        else:
            tok = (sem, self.cnt[sem] + 1, False, True)
            self.pseudo[sem] = max(self.pseudo.get(sem, 0), self.cnt[sem] + 1)
        self.ops[eng].append(("op", build, inc, sem))
        self.n_instr += 1
        self._post(tok, reads, writes)
        return tok

    def fence_one(self, eng):
        for s in list(self.cnt):
            if s == ENGSEM[eng]:
                continue
            if self.cnt[s] > self.waited[eng].get(s, 0):
                self.waited[eng][s] = self.cnt[s]
                self.ops[eng].append(("w", s, self.cnt[s]))

    def fview(self, donor, key, shape, off):
        if donor not in self._sb:
            self.sbuf(donor, [128, 1025], F32)
        ah, _ = self._sb[donor]
        if key not in self._ab:
            self._ab[key] = Buf(key)
        b = self._ab[key]
        flat = ah[0:shape[0], :].rearrange("p a -> p (a)") if False else ah
        if len(shape) == 2:
            v = ah[0:shape[0], off:off + shape[1]]
        else:
            v = ah[0:shape[0], off:off + shape[1] * shape[2]].rearrange(
                "p (a b) -> p a b", a=shape[1])
        return v, b

    def arena(self, key, shape, off):
        ah, _ = self.sbuf("arenaD", [128, self.arena_elems], BF16)
        if key not in self._ab:
            self._ab[key] = Buf(key)
        b = self._ab[key]
        if len(shape) == 2:
            v = ah[0:shape[0], off:off + shape[1]]
        else:
            v = ah[0:shape[0], off:off + shape[1] * shape[2]].rearrange(
                "p (a b) -> p a b", a=shape[1])
        return v, b

    def fence(self):
        for eng in ENG2BLK:
            self.fence_one(eng)

    def _dma_sem(self, writes):
        name = writes[0].name if writes else "_q"
        if name not in self.dma_sems:
            if len(self.dma_sems) >= 48:
                self.dma_sems[name] = "dov"
                if "dov" not in self.sems:
                    self.sems["dov"] = self.stack.enter_context(self.nc.semaphore("dov"))
                    self.cnt["dov"] = 0
            else:
                key = "d%d" % len(self.dma_sems)
                self.dma_sems[name] = key
                self.sems[key] = self.stack.enter_context(self.nc.semaphore(key))
                self.cnt[key] = 0
        return self.dma_sems[name]

    def dma(self, queue, out_ap, in_ap, reads=(), writes=()):
        if getattr(self, "serial", False):
            self.fence()
        elif queue in getattr(self, "serial_engines", ()):
            self.fence_one(queue)
        self._pre(queue, reads, writes)
        sem = self._dma_sem(writes)
        self.cnt[sem] += 16
        tok = (sem, self.cnt[sem], True)
        self.ops[queue].append(("op", lambda e, o=out_ap, i=in_ap: e.dma_start(out=o, in_=i), 16, sem))
        self.n_instr += 1
        self._post(tok, reads, writes)
        return tok

    def drain_dma(self, eng="sp"):
        for key in set(self.dma_sems.values()):
            if self.cnt.get(key, 0) > self.waited[eng].get(key, 0):
                self.waited[eng][key] = self.cnt[key]
                self.ops[eng].append(("w", key, self.cnt[key]))

    def mm(self, ps_ap, lhsT, rhs, reads, wbuf, start=True, stop=True, inc=None):
        if inc is None:
            inc = 1 if stop else 0
        return self.raw("pe", lambda e, p=ps_ap, l=lhsT, r=rhs, s1=start, s2=stop:
                        e.matmul(p, l, r, start=s1, stop=s2),
                        reads=reads, writes=[wbuf], inc=inc)

    def transpose(self, ps_ap, in_ap, ident_ap, reads, wbuf):
        return self.raw("pe", lambda e, p=ps_ap, i=in_ap, d=ident_ap:
                        e.transpose(p, i, d), reads=reads, writes=[wbuf])

    def tt(self, out, in0, in1, op, reads, writes, eng="dve"):
        return self.raw(eng, lambda e, o=out, a=in0, b=in1, p=op:
                        e.tensor_tensor(o, a, b, p), reads=reads, writes=writes)

    def ts(self, out, in0, s1, op0, reads, writes, s2=None, op1=ALU.bypass, eng="dve"):
        return self.raw(eng, lambda e, o=out, a=in0, x=s1, y=s2, p0=op0, p1=op1:
                        e.tensor_scalar(o, a, x, y, p0, p1), reads=reads, writes=writes)

    def stt(self, out, in0, scal, in1, op0, op1, reads, writes, eng="dve"):
        return self.raw(eng, lambda e, o=out, a=in0, s=scal, b=in1, p0=op0, p1=op1:
                        e.scalar_tensor_tensor(o, a, s, b, p0, p1), reads=reads, writes=writes)

    def act(self, out, in_, func, reads, writes, bias=0.0, scale=1.0):
        return self.raw("act", lambda e, o=out, i=in_, f=func, b=bias, s=scale:
                        e.activation(o, i, f, bias=b, scale=s), reads=reads, writes=writes)

    def amul(self, out, in_, mul, reads, writes):
        return self.raw("act", lambda e, o=out, i=in_, m=mul: e.mul(o, i, m),
                        reads=reads, writes=writes)

    def aadd(self, out, in_, add, reads, writes):
        return self.raw("act", lambda e, o=out, i=in_, a=add: e.add(o, i, a),
                        reads=reads, writes=writes)

    def cp(self, out, in_, reads, writes, eng="dve"):
        if eng == "act":
            return self.raw("act", lambda e, o=out, i=in_: e.copy(o, i), reads=reads, writes=writes)
        return self.raw(eng, lambda e, o=out, i=in_: e.tensor_copy(o, i), reads=reads, writes=writes)

    def scan(self, out, d0, d1, init, op0, op1, reads, writes, eng="dve"):
        return self.raw(eng, lambda e, o=out, a=d0, b=d1, i=init, p0=op0, p1=op1:
                        e.tensor_tensor_scan(o, a, b, i, p0, p1), reads=reads, writes=writes)

    def sel(self, out, mask, a, b, reads, writes):
        return self.raw("dve", lambda e, o=out, m=mask, x=a, y=b:
                        e.select(o, m, x, y), reads=reads, writes=writes)

    def memset(self, ap, val, writes, eng="pool"):
        return self.raw(eng, lambda e, a=ap, v=val: e.memset(a, v), writes=writes)

    def finalize(self):
        for eng, sem in ENGSEM.items():
            if sem and self.pseudo.get(sem, 0) > self.cnt[sem]:
                self.raw(eng, lambda e: e.nop(), inc=1)
        blk = self.stack.enter_context(self.nc.Block())
        for key, blkname in ENG2BLK.items():
            ops = self.ops[key]
            if not ops:
                continue

            def play(e, ops=ops):
                for item in ops:
                    if item[0] == "w":
                        e.wait_ge(self.sems[item[1]], item[2])
                    else:
                        inst = item[1](e)
                        if item[2]:
                            inst.then_inc(self.sems[item[3]], item[2])
            getattr(blk, blkname)(play)



INPUT_SPECS = {
    "x": ([2, CD, TB + 1], F32),
    "ident": ([128, 128], F32), "ones": ([128, 128], F32),
    "masksl": ([128, 128], F32), "maskadd": ([128, 128], F32),
    "tri_incl": ([128, 128], F32), "triu_strict": ([128, 128], F32),
    "maacols": ([128, 6, 16], F32), "tdecay": ([128, 16], F32),
    "faaaa": ([128, 16], F32), "lng": ([128, 16], F32), "lnb": ([128, 16], F32),
    "amask": ([128, 8], F32), "cmask": ([128, 8], F32),
    "lngrow": ([1, CD], F32), "lnbrow": ([1, CD], F32), "masksl2": ([64, 128], F32),
    "maa_w1": ([CD, 160], F32), "maa_w2": ([5, 32, CD], F32),
    "dw1": ([CD, 64], F32), "dw2": ([64, CD], F32),
    "WrT": ([CD, CD], F32), "WkT": ([CD, CD], F32), "WvT": ([CD, CD], F32),
    "WgT": ([CD, CD], F32), "WoT": ([CD, CD], F32),
    "f0gate": ([CD, 8192], F32), "f0up": ([CD, 8192], F32), "f0down": ([8192, CD], F32),
    "winT": ([CD, 8192], F32), "wparamT": ([CD, 8320], F32), "woutT": ([4096, CD], F32),
    "f1gateT": ([CD, 8192], F32), "f1upT": ([CD, 8192], F32), "f1downT": ([8192, CD], F32),
}

SCRATCH_SPECS = {
    "xn": ([CD, TPC], BF16), "xxn": ([CD, TPC], BF16),
    "kb": ([CD, TPC], BF16), "rb": ([CD, TPC], BF16),
    "vt": ([TPC, CD], BF16), "gt": ([TPC, CD], BF16),
    "logd": ([CD, TPC], F32),
    "yloc": ([TPC, CD], BF16), "rt": ([CD, TPC], BF16),
    "dpref": ([128, KC, 2 * NCHB], F32),
    "sloc": ([2, 128, KC, 64], F32), "ldtot": ([2, 128, KC], F32),
    "slocg": ([NCORES, 2, 128, KC, 64], F32), "ldtotg": ([NCORES, 2, 128, KC], F32),
    "s0": ([CD, TPC], BF16), "out0": ([CD, TPC], F32),
    "xn1": ([CD, TPC], BF16),
    "Xs": ([16, 64, 8192], BF16), "Bs": ([16, 64, 8192], BF16),
    "Cs": ([16, 64, 8192], BF16), "Rs": ([16, 64, 8192], BF16),
    "yy2": ([16, 128, 8256], BF16),
    "s1": ([CD, TPC], BF16),
}

OUTPUT_SPECS = {"y": ([2, CD, TB], F32)}


def declare(nc, names, kind):
    T = {}
    for n in names:
        spec = INPUT_SPECS.get(n) or SCRATCH_SPECS.get(n) or OUTPUT_SPECS.get(n)
        T[n] = nc.dram_tensor(n, spec[0], spec[1], kind=kind).ap()
    return T



def _col16(v):
    return np.ascontiguousarray(np.asarray(v, np.float32).reshape(16, 128).T)


def host_prep(inputs):
    x = np.asarray(inputs["L0_attn_x"], np.float32)
    shared = {}
    g = lambda k: np.asarray(inputs[k], np.float32)

    s_idx = np.arange(128)[:, None]
    t_idx = np.arange(128)[None, :]
    shared["ident"] = np.eye(128, dtype=np.float32)
    shared["ones"] = np.ones((128, 128), np.float32)
    shared["masksl"] = (s_idx < t_idx).astype(np.float32)
    shared["maskadd"] = np.where(s_idx <= t_idx, 0.0, NEG).astype(np.float32)
    shared["tri_incl"] = (s_idx <= t_idx).astype(np.float32)
    shared["triu_strict"] = (s_idx > t_idx).astype(np.float32)

    maa = np.stack([g("L0_attn_time_maa_" + n).reshape(-1) for n in
                    ("x", "w", "k", "v", "r", "g")])
    shared["maacols"] = np.ascontiguousarray(np.stack([_col16(m) for m in maa], 1))
    shared["tdecay"] = _col16(g("L0_attn_time_decay").reshape(-1))
    shared["faaaa"] = _col16(g("L0_attn_time_faaaa").reshape(-1))
    shared["lng"] = _col16(g("L0_attn_ln_gamma"))
    shared["lnb"] = _col16(g("L0_attn_ln_beta"))
    shared["lngrow"] = np.ascontiguousarray(g("L0_attn_ln_gamma").reshape(1, -1))
    shared["lnbrow"] = np.ascontiguousarray(g("L0_attn_ln_beta").reshape(1, -1))
    shared["masksl2"] = np.ascontiguousarray(
        np.tile((np.arange(64)[:, None] < np.arange(64)[None, :]).astype(np.float32), (1, 2)))
    shared["maa_w1"] = g("L0_attn_time_maa_w1")
    shared["maa_w2"] = np.ascontiguousarray(g("L0_attn_time_maa_w2"))
    shared["dw1"] = g("L0_attn_time_decay_w1")
    shared["dw2"] = g("L0_attn_time_decay_w2")
    for n in ("r", "k", "v", "g", "o"):
        shared["W%sT" % n] = np.ascontiguousarray(g("L0_attn_W_" + n).T)

    jj = np.arange(8192)
    perm = ((jj // 512) * 128 + (jj % 128)) * 4 + (jj % 512) // 128
    shared["f0gate"] = np.ascontiguousarray(g("L0_ffn_w_gate")[:, perm])
    shared["f0up"] = np.ascontiguousarray(g("L0_ffn_w_up")[:, perm])
    shared["f0down"] = np.ascontiguousarray(g("L0_ffn_w_down")[perm, :])

    shared["winT"] = np.ascontiguousarray(g("L1_attn_w_in").T)
    wp = g("L1_attn_w_param").T
    dcols = np.arange(64) * 129
    bcols = (np.arange(64)[:, None] * 129 + 1 + np.arange(64)[None, :]).reshape(-1)
    ccols = (np.arange(64)[:, None] * 129 + 65 + np.arange(64)[None, :]).reshape(-1)
    shared["wparamT"] = np.ascontiguousarray(wp[:, np.concatenate([dcols, dcols, bcols, ccols])])
    shared["woutT"] = np.ascontiguousarray(g("L1_attn_w_out").T)
    shared["f1gateT"] = np.ascontiguousarray(g("L1_ffn_w_gate").T)
    shared["f1upT"] = np.ascontiguousarray(g("L1_ffn_w_up").T)
    shared["f1downT"] = np.ascontiguousarray(g("L1_ffn_w_down").T)

    in_maps = []
    for ci in range(NCORES):
        m = dict(shared)
        xc = np.zeros((2, CD, TB + 1), np.float32)
        lo = ci * TB
        xc[:, :, 1:] = x[:, lo:lo + TB, :].transpose(0, 2, 1)
        if ci > 0:
            xc[:, :, 0] = x[:, lo - 1, :]
        m["x"] = np.ascontiguousarray(xc)
        am = np.zeros((128, 8), np.float32)
        cm = np.full((128, 8), NEG, np.float32)
        am[:, :ci] = 1.0
        cm[:, :ci] = 0.0
        m["amask"] = np.ascontiguousarray(am)
        m["cmask"] = np.ascontiguousarray(cm)
        in_maps.append(m)
    return in_maps


CONST_SPECS = [
    ("ident_f", [128, 128], F32, "ident"), ("ident_b", [128, 128], BF16, "ident"),
    ("ones_f", [128, 128], F32, "ones"), ("ones_b", [128, 128], BF16, "ones"),
    ("masksl_b", [128, 128], BF16, "masksl"), ("maskadd_f", [128, 128], F32, "maskadd"),
    ("tri_incl_b", [128, 128], BF16, "tri_incl"), ("triu_strict_b", [128, 128], BF16, "triu_strict"),
    ("tri_incl_f", [128, 128], F32, "tri_incl"), ("triu_strict_f", [128, 128], F32, "triu_strict"),
    ("maacols", [128, 6, 16], F32, "maacols"), ("tdecay", [128, 16], F32, "tdecay"),
    ("faaaa_b", [128, 16], BF16, "faaaa"), ("faaaa_f", [128, 16], F32, "faaaa"), ("lng_f", [128, 16], F32, "lng"),
    ("lnb_f", [128, 16], F32, "lnb"),
    ("masksl2_b", [64, 128], BF16, "masksl2"),
    ("lngrow_b", [1, CD], BF16, "lngrow"), ("lnbrow_b", [1, CD], BF16, "lnbrow"),
    ("amask", [128, 8], F32, "amask"), ("cmask", [128, 8], F32, "cmask"),
]


def load_consts(P, T, names=None):
    cb = {}
    for name, shape, dt, src in CONST_SPECS:
        if names is not None and name not in names:
            continue
        if src not in T:
            continue
        h, b = P.sbuf("c_" + name, shape, dt)
        q = "pool" if dt == BF16 else "sp"
        P.dma(q, h[:], T[src], writes=[b])
        cb[name] = h
        cb[name + "/b"] = b
    return cb



def stage_a(P, T, cb):
    segs = [(0, 512), (512, 512), (1024, 1)]
    xt = [P.sbuf("xtA%d" % i, [128, 1025], F32) for i in range(2)]
    sq_h, sq_b = P.sbuf("sqA", [128, 1025], F32)
    mrow_h, mrow_b = P.sbuf("mrowA", [1, 2, 1025], F32)
    rrow_h, rrow_b = P.sbuf("rrowA", [1, 2, 1025], F32)
    vrow_h, vrow_b = P.sbuf("vrowA", [1, 1025], F32)
    mb_h, mb_b = P.sbuf("mbA", [128, 1025], F32)
    rb_h, rb_b = P.sbuf("rbA", [128, 1025], F32)
    xnt_h, xnt_b = P.sbuf("xntA", [128, 1025], BF16)
    xxt_h, xxt_b = P.sbuf("xxtA", [128, 1024], BF16)
    tmpf_h, tmpf_b = P.sbuf("tmpfA", [128, 1025], F32)

    ps = [P.psum("ps%d" % i, [128, 512], F32) for i in range(6)]
    eps_h, eps_b = P.sbuf("epsA", [128, 1], F32)
    P.memset(eps_h[:], 1e-5, writes=[eps_b])

    for b in range(2):
        for cc in range(KC):
            xh, xb = xt[cc % 2]
            P.dma("sp", xh[:], T["x"][b, cc * 128:(cc + 1) * 128, :], writes=[xb])
            P.act(sq_h[:], xh[:], AF.Square, reads=[xb], writes=[sq_b])
            for si, (o, w) in enumerate(segs):
                psh, psb = ps[si]
                pqh, pqb = ps[3 + si]
                P.mm(psh[0:1, 0:w], cb["ones_f"][:, 0:1], xh[:, o:o + w],
                     reads=[xb, cb["ones_f/b"]], wbuf=psb, start=(cc == 0), stop=(cc == KC - 1),
                     inc=1)
                P.mm(pqh[0:1, 0:w], cb["ones_f"][:, 0:1], sq_h[:, o:o + w],
                     reads=[sq_b, cb["ones_f/b"]], wbuf=pqb, start=(cc == 0), stop=(cc == KC - 1),
                     inc=1)
        for si, (o, w) in enumerate(segs):
            psh, psb = ps[si]
            pqh, pqb = ps[3 + si]
            P.amul(mrow_h[0:1, b, o:o + w], psh[0:1, 0:w], 1.0 / CD, reads=[psb], writes=[mrow_b])
            P.amul(vrow_h[0:1, o:o + w], pqh[0:1, 0:w], 1.0 / CD, reads=[pqb], writes=[vrow_b])
        P.stt(tmpf_h[0:1, 0:1025], mrow_h[0:1, b, :], -1.0, mrow_h[0:1, b, :], ALU.mult, ALU.mult,
              reads=[mrow_b], writes=[tmpf_b])
        P.tt(vrow_h[0:1, :], vrow_h[0:1, :], tmpf_h[0:1, 0:1025], ALU.add,
             reads=[vrow_b, tmpf_b], writes=[vrow_b])
        P.act(vrow_h[0:1, :], vrow_h[0:1, :], AF.Ln, reads=[vrow_b, eps_b], writes=[vrow_b],
              bias=eps_h[0:1, :])
        P.act(rrow_h[0:1, b, :], vrow_h[0:1, :], AF.Exp, reads=[vrow_b], writes=[rrow_b], scale=-0.5)
    for b in range(2):
        for si, (o, w) in enumerate(segs):
            psh, psb = ps[si]
            pqh, pqb = ps[3 + si]
            P.mm(psh[:, 0:w], cb["ones_f"][0:1, :], mrow_h[0:1, b, o:o + w],
                 reads=[mrow_b, cb["ones_f/b"]], wbuf=psb)
            P.cp(mb_h[:, o:o + w], psh[:, 0:w], reads=[psb], writes=[mb_b], eng="act")
            P.mm(pqh[:, 0:w], cb["ones_f"][0:1, :], rrow_h[0:1, b, o:o + w],
                 reads=[rrow_b, cb["ones_f/b"]], wbuf=pqb)
            P.cp(rb_h[:, o:o + w], pqh[:, 0:w], reads=[pqb], writes=[rb_b], eng="act")
        for cc in range(KC):
            xh, xb = xt[cc % 2]
            P.dma("sp", xh[:], T["x"][b, cc * 128:(cc + 1) * 128, :], writes=[xb])
            P.tt(tmpf_h[:], xh[:], mb_h[:], ALU.subtract, reads=[xb, mb_b], writes=[tmpf_b])
            P.tt(xnt_h[:], tmpf_h[:], rb_h[:], ALU.mult, reads=[tmpf_b, rb_b], writes=[xnt_b])
            P.tt(xxt_h[:], xnt_h[:, 0:1024], xnt_h[:, 1:1025], ALU.subtract,
                 reads=[xnt_b], writes=[xxt_b])
            r0 = cc * 128
            P.dma("sp", T["xn"][r0:r0 + 128, b * TB:(b + 1) * TB], xnt_h[:, 1:1025], reads=[xnt_b])
            P.dma("sp", T["xxn"][r0:r0 + 128, b * TB:(b + 1) * TB], xxt_h[:], reads=[xxt_b])

    mix = [P.arena("mix%d" % i, [128, KC, 512], 16384 + 8192 * i) for i in range(2)]
    w1_h, w1_b = P.sbuf("w1A", [128, KC, 160], BF16)
    a_h, a_b = P.arena("aA", [32, 5, 2048], 40960)
    xnc = [P.sbuf("xncA%d" % i, [128, 512], BF16) for i in range(2)]
    xxc = [P.sbuf("xxcA%d" % i, [128, 512], BF16) for i in range(2)]
    t1_h, t1_b = P.sbuf("t1A", [128, 512], BF16)
    t2_h, t2_b = P.sbuf("t2A", [128, 512], BF16)

    P.dma("pool", w1_h[:], T["maa_w1"].rearrange("(c p) o -> p c o", p=128), writes=[w1_b])

    def build_mix(tb, mi, dst, dstb, w2pair=None):
        sl = slice(tb * 512, (tb + 1) * 512)
        for cc in range(KC):
            r0 = cc * 128
            xh, xb = xnc[cc % 2]
            yh, yb = xxc[cc % 2]
            P.dma("sp", xh[:], T["xn"][r0:r0 + 128, sl], writes=[xb])
            P.dma("sp", yh[:], T["xxn"][r0:r0 + 128, sl], writes=[yb])
            if w2pair is None:
                P.amul(t1_h[:], yh[:], cb["maacols"][:, mi, cc:cc + 1],
                       reads=[yb, cb["maacols/b"]], writes=[t1_b])
                P.tt(dst[:, cc, :], xh[:], t1_h[:], ALU.add, reads=[xb, t1_b], writes=[dstb])
            else:
                w2h, w2bf, fi, psh, psb = w2pair
                P.mm(psh[:, 0:512], w2h[:, cc, :], a_h[:, fi, sl],
                     reads=[w2bf, a_b], wbuf=psb)
                P.aadd(t1_h[:], psh[:, 0:512], cb["maacols"][:, mi, cc:cc + 1],
                       reads=[psb, cb["maacols/b"]], writes=[t1_b])
                P.tt(t2_h[:], t1_h[:], yh[:], ALU.mult, reads=[t1_b, yb], writes=[t2_b])
                P.tt(dst[:, cc, :], xh[:], t2_h[:], ALU.add, reads=[xb, t2_b], writes=[dstb])

    for tb in range(4):
        mh, mb_ = mix[tb % 2]
        build_mix(tb, 0, mh, mb_)
        for f in range(5):
            psh, psb = ps[2 + f % 2]
            for cc in range(KC):
                P.mm(psh[0:32, 0:512], w1_h[:, cc, f * 32:(f + 1) * 32], mh[:, cc, :],
                     reads=[w1_b, mb_], wbuf=psb, start=(cc == 0), stop=(cc == KC - 1))
            P.act(a_h[:, f, tb * 512:(tb + 1) * 512], psh[0:32, 0:512], AF.Tanh,
                  reads=[psb], writes=[a_b])

    if "dbga" in T:
        P.dma("sp", T["dbga"], a_h[:], reads=[a_b])
    bigB_h, bigB_b = P.arena("bigB", [128, KC, 1024], 0)
    w2_h, w2_b = P.sbuf("w2A", [32, KC, 128], BF16)
    dw1_h, dw1_b = P.sbuf("dw1A", [128, KC, 64], BF16)
    dw2_h, dw2_b = P.sbuf("dw2A", [64, KC, 128], BF16)
    d1_h, d1_b = P.sbuf("d1A", [64, 2048], BF16)
    stg_h, stg_b = P.arena("stgAr", [128, KC, 512], 32768)
    tmf_h, tmf_b = P.sbuf("tmfA", [128, 512], F32)

    P.dma("pool", dw1_h[:], T["dw1"].rearrange("(c p) o -> p c o", p=128), writes=[dw1_b])
    P.dma("pool", dw2_h[:], T["dw2"].rearrange("d (c p) -> d c p", p=128), writes=[dw2_b])

    projs = [("w", 1, 0, None, None, "w"),
             ("k", 2, 1, "WkT", "kb", "fm"),
             ("v", 3, 2, "WvT", "vt", "tm"),
             ("r", 4, 3, "WrT", "rb", "fm"),
             ("g", 5, 4, "WgT", "gt", "tm")]
    for pname, mi, fi, wkey, okey, mode in projs:
        P.dma("pool", w2_h[:], T["maa_w2"][fi].rearrange("d (c p) -> d c p", p=128), writes=[w2_b])
        nhf = 1 if mode == "w" else 2
        for hf in range(nhf):
            P.fence()
            if wkey is not None:
                for q in range(2):
                    P.dma("pool", bigB_h[:, 8 * q:8 * (q + 1), :],
                          T[wkey].rearrange("(c p) o -> p c o", p=128)[:, 8 * q:8 * (q + 1),
                                                                      hf * 1024:(hf + 1) * 1024],
                          writes=[bigB_b])
            for tb in range(4):
                mh, mb_ = mix[tb % 2]
                build_mix(tb, mi, mh, mb_, w2pair=(w2_h, w2_b, fi, ps[4][0], ps[4][1]))
                if "dbgmix" in T and pname == "k" and hf == 0:
                    P.dma("sp", T["dbgmix"][tb], mh[:], reads=[mb_])
                sl = slice(tb * 512, (tb + 1) * 512)
                if mode == "fm":
                    for ocl in range(8):
                        oc = hf * 8 + ocl
                        psh, psb = ps[ocl % 2]
                        for cc in range(KC):
                            P.mm(psh[:, 0:512], bigB_h[:, cc, ocl * 128:(ocl + 1) * 128], mh[:, cc, :],
                                 reads=[bigB_b, mb_], wbuf=psb, start=(cc == 0), stop=(cc == KC - 1))
                        P.cp(stg_h[:, oc, :], psh[:, 0:512], reads=[psb], writes=[stg_b], eng="act")
                    P.dma("sp", T[okey].rearrange("(oc p) t -> p oc t", p=128)[:, hf * 8:(hf + 1) * 8, sl],
                          stg_h[:, hf * 8:(hf + 1) * 8, :], reads=[stg_b])
                elif mode == "tm":
                    for tc4 in range(4):
                        tc = tb * 4 + tc4
                        for ob in range(2):
                            psh, psb = ps[ob % 2]
                            for cc in range(KC):
                                P.mm(psh[:, 0:512], mh[:, cc, tc4 * 128:(tc4 + 1) * 128],
                                     bigB_h[:, cc, ob * 512:(ob + 1) * 512],
                                     reads=[bigB_b, mb_], wbuf=psb, start=(cc == 0), stop=(cc == KC - 1))
                            if pname == "g":
                                P.act(stg_h[:, ob, :], psh[:, 0:512], AF.Silu, reads=[psb], writes=[stg_b])
                            else:
                                P.cp(stg_h[:, ob, :], psh[:, 0:512], reads=[psb], writes=[stg_b], eng="act")
                        P.dma("sp", T[okey][tc * 128:(tc + 1) * 128, hf * 1024:(hf + 1) * 1024],
                              stg_h[:, 0:2, :].rearrange("p a b -> p (a b)"), reads=[stg_b])
                else:
                    psh, psb = ps[2]
                    for cc in range(KC):
                        P.mm(psh[0:64, 0:512], dw1_h[:, cc, :], mh[:, cc, :],
                             reads=[dw1_b, mb_], wbuf=psb, start=(cc == 0), stop=(cc == KC - 1))
                    P.act(d1_h[:, sl], psh[0:64, 0:512], AF.Tanh, reads=[psb], writes=[d1_b])
        if mode == "w":
            for tb in range(4):
                sl = slice(tb * 512, (tb + 1) * 512)
                for oc in range(KC):
                    psh, psb = ps[oc % 2]
                    P.mm(psh[:, 0:512], dw2_h[:, oc, :], d1_h[:, sl],
                         reads=[dw2_b, d1_b], wbuf=psb)
                    P.act(tmf_h[:], psh[:, 0:512], AF.Exp, reads=[psb, cb["tdecay/b"]],
                          writes=[tmf_b], bias=cb["tdecay"][:, oc:oc + 1])
                    P.ts(tmf_h[:], tmf_h[:], -1.0, ALU.mult, reads=[tmf_b], writes=[tmf_b])
                    P.dma("sp", T["logd"][oc * 128:(oc + 1) * 128, sl], tmf_h[:], reads=[tmf_b])



def stage_b(P, T, cb):
    import os
    LVL = int(os.environ.get("BLVL", "9"))
    P.fence()
    kc_h, kc_b = P.arena("kcB", [128, KC, QW], 0)
    rc_h, rc_b = P.arena("rcB", [128, KC, QW], 1024)
    ldc_h, ldc_b = P.fview("mbA", "ldcB", [128, KC, QW], 0)
    vtc_h, vtc_b = P.arena("vtcB", [QW, CD], 7168)
    cs_h, cs_b = P.fview("rbA", "csB", [128, KC, QW], 0)
    e1_h, e1_b = P.fview("tmpfA", "e1B", [128, KC, QW], 0)
    rte_h, rte_b = P.arena("rteB", [128, KC, QW], 2048)
    kte_h, kte_b = P.arena("kteB", [128, KC, QW], 3072)
    kdd_h, kdd_b = P.arena("kddB", [128, KC, QW], 4096)
    ruk_h, ruk_b = P.arena("rukB", [128, KC, QW], 5120)
    expd_h, expd_b = P.fview("xtA1", "expdB", [128, KC, 1], 0)
    run_h, run_b = P.fview("xtA1", "runB", [128, KC, 1], 16)
    dpre_h, dpre_b = P.fview("xtA0", "dpreB", [128, KC, 2 * NCHB], 0)
    kdt_h, kdt_b = P.arena("kdtB", [64, KC, 128], 9216)
    Mst_h, Mst_b = P.arena("MstB", [64, KC, 128], 11264)
    du_h, du_b = P.fview("xtA1", "duB", [64, KC, 2], 32)
    yub_h, yub_b = P.arena("yubB", [64, KC, 128], 13312)
    c1_h, c1_b = P.arena("c1B", [64, KC, 128], 15360)
    yst_h, yst_b = P.arena("ystB", [64, CD], 17408)
    S_h, S_b = P.fview("sqA", "SB", [128, KC, 64], 0)
    Sb_h, Sb_b = P.arena("SbB", [128, KC, 64], 6144)

    ps = [P.psum("ps%d" % i, [128, 512], F32) for i in range(6)]
    pb0_h, pb0_b = P.psum("pb0", [128, 1024], BF16)

    kb_r = T["kb"].rearrange("(c p) t -> p c t", p=128)
    rb_r = T["rb"].rearrange("(c p) t -> p c t", p=128)
    ld_r = T["logd"].rearrange("(c p) t -> p c t", p=128)
    rt_r = T["rt"].rearrange("(c p) t -> p c t", p=128)

    for b in range(2):
        P.memset(S_h[:], 0.0, writes=[S_b])
        P.memset(Sb_h[:], 0.0, writes=[Sb_b])
        P.memset(run_h[:], 0.0, writes=[run_b])
        for ch in range(NCHB):
            col0 = b * TB + ch * QW
            csl = slice(col0, col0 + QW)
            import os as _os
            if _os.environ.get("PLAINLD", "0") == "1":
                for cc in range(KC):
                    P.dma("sp", kc_h[:, cc, :], T["kb"][cc*128:(cc+1)*128, csl], writes=[kc_b])
                    P.dma("sp", rc_h[:, cc, :], T["rb"][cc*128:(cc+1)*128, csl], writes=[rc_b])
                    P.dma("sp", ldc_h[:, cc, :], T["logd"][cc*128:(cc+1)*128, csl], writes=[ldc_b])
            else:
                P.dma("sp", kc_h[:], kb_r[:, :, csl], writes=[kc_b])
                P.dma("sp", rc_h[:], rb_r[:, :, csl], writes=[rc_b])
                P.dma("sp", ldc_h[:], ld_r[:, :, csl], writes=[ldc_b])
            P.dma("sp", vtc_h[:], T["vt"][csl, :], writes=[vtc_b])
            for cc in range(KC):
                P.scan(cs_h[:, cc, :], ldc_h[:, cc, :], ldc_h[:, cc, :], 0.0,
                       ALU.add, ALU.bypass, reads=[ldc_b], writes=[cs_b])
            idx = b * NCHB + ch
            P.act(dpre_h[:, :, idx:idx + 1], run_h[:], AF.Exp, reads=[run_b], writes=[dpre_b])
            if LVL < 2:
                P.tt(run_h[:], run_h[:], cs_h[:, :, QW - 1:QW], ALU.add,
                     reads=[run_b, cs_b], writes=[run_b])
                continue
            P.tt(run_h[:], run_h[:], cs_h[:, :, QW - 1:QW], ALU.add,
                 reads=[run_b, cs_b], writes=[run_b])
            P.tt(e1_h[:], cs_h[:], ldc_h[:], ALU.subtract, reads=[cs_b, ldc_b], writes=[e1_b])
            P.act(e1_h[:], e1_h[:], AF.Exp, reads=[e1_b], writes=[e1_b])
            P.tt(rte_h[:], rc_h[:], e1_h[:], ALU.mult, reads=[rc_b, e1_b], writes=[rte_b])
            P.dma("sp", rt_r[:, :, csl], rte_h[:], reads=[rte_b])
            P.act(e1_h[:], cs_h[:], AF.Exp, reads=[cs_b], writes=[e1_b], scale=-1.0)
            P.tt(kte_h[:], kc_h[:], e1_h[:], ALU.mult, reads=[kc_b, e1_b], writes=[kte_b])
            for cc in range(KC):
                P.ts(e1_h[:, cc, :], cs_h[:, cc, :], cs_h[:, cc, QW - 1:QW], ALU.subtract,
                     reads=[cs_b], writes=[e1_b])
            P.act(e1_h[:], e1_h[:], AF.Exp, reads=[e1_b], writes=[e1_b], scale=-1.0)
            P.tt(kdd_h[:], kc_h[:], e1_h[:], ALU.mult, reads=[kc_b, e1_b], writes=[kdd_b])
            P.act(expd_h[:], cs_h[:, :, QW - 1:QW], AF.Exp, reads=[cs_b], writes=[expd_b])
            P.tt(ruk_h[:], rc_h[:], kc_h[:], ALU.mult, reads=[rc_b, kc_b], writes=[ruk_b])
            for cc in range(KC):
                P.ts(ruk_h[:, cc, :], ruk_h[:, cc, :], cb["faaaa_f"][:, cc:cc + 1], ALU.mult,
                     reads=[cb["faaaa_f/b"]], writes=[ruk_b])
            if LVL < 3:
                continue
            for cc in range(KC):
                P.transpose(pb0_h[0:64, 0:128], kdd_h[:, cc, :], cb["ident_b"][:],
                            reads=[kdd_b, cb["ident_b/b"]], wbuf=pb0_b)
                P.cp(kdt_h[:, cc, :], pb0_h[0:64, 0:128], reads=[pb0_b], writes=[kdt_b])
            if LVL < 4:
                continue
            for cc in range(KC):
                lo, hi = slice(0, 64), slice(64, 128)
                i0 = slice(2 * cc * 64, (2 * cc + 1) * 64)
                i1 = slice((2 * cc + 1) * 64, (2 * cc + 2) * 64)
                P.mm(ps[0][0][0:64, 0:64], kte_h[lo, cc, :], rte_h[lo, cc, :],
                     reads=[kte_b, rte_b], wbuf=ps[0][1])
                P.mm(ps[0][0][0:64, 64:128], kte_h[hi, cc, :], rte_h[hi, cc, :],
                     reads=[kte_b, rte_b], wbuf=ps[0][1])
                P.tt(Mst_h[:, cc, :], ps[0][0][0:64, 0:128], cb["masksl2_b"][:],
                     ALU.mult, reads=[ps[0][1], cb["masksl2_b/b"]], writes=[Mst_b])
                P.mm(ps[4][0][0:64, 0:1], ruk_h[lo, cc, :], cb["ones_b"][0:64, 0:1],
                     reads=[ruk_b, cb["ones_b/b"]], wbuf=ps[4][1])
                P.mm(ps[4][0][0:64, 1:2], ruk_h[hi, cc, :], cb["ones_b"][64:128, 0:1],
                     reads=[ruk_b, cb["ones_b/b"]], wbuf=ps[4][1])
                P.cp(du_h[:, cc, :], ps[4][0][0:64, 0:2], reads=[ps[4][1]], writes=[du_b])
                if LVL < 5:
                    continue
                P.mm(ps[1][0][0:64, 0:64], Mst_h[:, cc, 0:64], vtc_h[:, i0],
                     reads=[Mst_b, vtc_b], wbuf=ps[1][1])
                P.mm(ps[1][0][0:64, 64:128], Mst_h[:, cc, 64:128], vtc_h[:, i1],
                     reads=[Mst_b, vtc_b], wbuf=ps[1][1])
                P.mm(ps[2][0][0:64, 0:64], rte_h[lo, cc, :], Sb_h[lo, cc, :],
                     reads=[rte_b, Sb_b], wbuf=ps[2][1])
                P.mm(ps[2][0][0:64, 64:128], rte_h[hi, cc, :], Sb_h[hi, cc, :],
                     reads=[rte_b, Sb_b], wbuf=ps[2][1])
                P.ts(yub_h[:, cc, 0:64], vtc_h[:, i0], du_h[:, cc, 0:1], ALU.mult,
                     reads=[vtc_b, du_b], writes=[yub_b])
                P.ts(yub_h[:, cc, 64:128], vtc_h[:, i1], du_h[:, cc, 1:2], ALU.mult,
                     reads=[vtc_b, du_b], writes=[yub_b])
                P.stt(c1_h[:, cc, :], ps[1][0][0:64, 0:128], 1.0, yub_h[:, cc, :],
                      ALU.mult, ALU.add, reads=[ps[1][1], yub_b], writes=[c1_b])
                P.stt(yst_h[:, 2 * cc * 64:(2 * cc + 2) * 64], ps[2][0][0:64, 0:128], 1.0,
                      c1_h[:, cc, :], ALU.mult, ALU.add, reads=[ps[2][1], c1_b], writes=[yst_b])
                P.mm(ps[3][0][0:64, 0:64], kdt_h[:, cc, 0:64], vtc_h[:, i0],
                     reads=[kdt_b, vtc_b], wbuf=ps[3][1])
                P.mm(ps[3][0][64:128, 0:64], kdt_h[:, cc, 64:128], vtc_h[:, i1],
                     reads=[kdt_b, vtc_b], wbuf=ps[3][1])
                P.stt(S_h[:, cc, :], S_h[:, cc, :], expd_h[:, cc, 0:1], ps[3][0][:, 0:64],
                      ALU.mult, ALU.add, reads=[expd_b, ps[3][1]], writes=[S_b])
                P.cp(Sb_h[:, cc, :], S_h[:, cc, :], reads=[S_b], writes=[Sb_b])
            P.dma("sp", T["yloc"][csl, :], yst_h[:], reads=[yst_b])
        P.dma("sp", T["sloc"][b], S_h[:], reads=[S_b])
        P.dma("sp", T["ldtot"][b], run_h[:, :, 0], reads=[run_b])
    P.dma("sp", T["dpref"], dpre_h[:], reads=[dpre_b])



def stage_c(P, T, cb):
    ps = [P.psum("ps%d" % i, [128, 512], F32) for i in range(6)]
    pb0_h, pb0_b = P.psum("pb0", [128, 1024], BF16)

    sin_h, sin_b = P.sbuf("sinC", [128, 2, KC, 64], F32)
    slm_h, slm_b = P.sbuf("slmC", [128, KC, 64], F32)
    ldm_h, ldm_b = P.sbuf("ldmC", [128, KC], F32)
    acc_h, acc_b = P.sbuf("accC", [128, 2, KC], F32)
    coef_h, coef_b = P.sbuf("coefC", [128, KC], F32)
    P.memset(sin_h[:], 0.0, writes=[sin_b])
    P.memset(acc_h[:], 0.0, writes=[acc_b])
    for m in range(NCORES - 1, -1, -1):
        for b in range(2):
            P.dma("sp", slm_h[:], T["slocg"][m, b], writes=[slm_b])
            P.dma("sp", ldm_h[:], T["ldtotg"][m, b], writes=[ldm_b])
            P.ts(coef_h[:], acc_h[:, b, :], cb["cmask"][:, m:m + 1], ALU.add,
                 reads=[acc_b, cb["cmask/b"]], writes=[coef_b])
            P.act(coef_h[:], coef_h[:], AF.Exp, reads=[coef_b], writes=[coef_b])
            for cc in range(KC):
                P.stt(sin_h[:, b, cc, :], slm_h[:, cc, :], coef_h[:, cc:cc + 1],
                      sin_h[:, b, cc, :], ALU.mult, ALU.add,
                      reads=[slm_b, coef_b], writes=[sin_b])
            P.stt(acc_h[:, b, :], ldm_h[:], cb["amask"][:, m:m + 1], acc_h[:, b, :],
                  ALU.mult, ALU.add, reads=[ldm_b, cb["amask/b"]], writes=[acc_b])

    dpre_h, dpre_b = P.sbuf("dpreB", [128, KC, 2 * NCHB], F32)
    P.dma("sp", dpre_h[:], T["dpref"], writes=[dpre_b])

    garow_h, garow_b = P.sbuf("garowC", [64, CD], BF16)
    bbrow_h, bbrow_b = P.sbuf("bbrowC", [64, CD], BF16)
    for q in range(4):
        sl = slice(q * 512, (q + 1) * 512)
        P.mm(ps[0][0][0:64, 0:512], cb["ones_b"][0:1, 0:64], cb["lngrow_b"][0:1, sl],
             reads=[cb["ones_b/b"], cb["lngrow_b/b"]], wbuf=ps[0][1])
        P.cp(garow_h[:, sl], ps[0][0][0:64, 0:512], reads=[ps[0][1]], writes=[garow_b], eng="act")
        P.mm(ps[1][0][0:64, 0:512], cb["ones_b"][0:1, 0:64], cb["lnbrow_b"][0:1, sl],
             reads=[cb["ones_b/b"], cb["lnbrow_b/b"]], wbuf=ps[1][1])
        P.cp(bbrow_h[:, sl], ps[1][0][0:64, 0:512], reads=[ps[1][1]], writes=[bbrow_b], eng="act")
    epsg_h, epsg_b = P.sbuf("epsgC", [64, 1], F32)
    P.memset(epsg_h[:], 1e-5 * 64.0, writes=[epsg_b])

    woB_h, woB_b = P.sbuf("woB", [128, KC, 2048], BF16)
    for q in range(4):
        P.dma("pool", woB_h[:, 4 * q:4 * (q + 1), :],
              T["WoT"].rearrange("(c p) o -> p c o", p=128)[:, 4 * q:4 * (q + 1), :],
              writes=[woB_b])

    ych_h, ych_b = P.sbuf("ychC", [64, CD], BF16)
    ysq_h, ysq_b = P.sbuf("ysqC", [64, CD], F32)
    gch_h, gch_b = P.sbuf("gchC", [64, CD], BF16)
    rtc_h, rtc_b = P.sbuf("rtcC", [128, KC, QW], BF16)
    shat_h, shat_b = P.sbuf("shatC", [128, KC, 64], BF16)
    ym_h, ym_b = P.sbuf("ymC", [64, 32], F32)
    yv_h, yv_b = P.sbuf("yvC", [64, 32], F32)
    gt1_h, gt1_b = P.sbuf("gt1C", [64, 32], F32)
    gnr_h, gnr_b = P.sbuf("gnrC", [64, 32], F32)
    ygf_h, ygf_b = P.sbuf("ygfC", [128, KC, 512], BF16)
    stg_h, stg_b = P.sbuf("stgA", [128, KC, 512], BF16)

    rt_r = T["rt"].rearrange("(c p) t -> p c t", p=128)
    for b in range(2):
        for ch in range(NCHB):
            idx = b * NCHB + ch
            col0 = b * TB + ch * QW
            csl = slice(col0, col0 + QW)
            P.dma("sp", ych_h[:], T["yloc"][csl, :], writes=[ych_b])
            P.dma("sp", rtc_h[:], rt_r[:, :, csl], writes=[rtc_b])
            P.dma("sp", gch_h[:], T["gt"][csl, :], writes=[gch_b])
            for cc in range(KC):
                P.ts(shat_h[:, cc, :], sin_h[:, b, cc, :], dpre_h[:, cc, idx:idx + 1],
                     ALU.mult, reads=[sin_b, dpre_b], writes=[shat_b])
            for cc in range(KC):
                lo, hi = slice(0, 64), slice(64, 128)
                pc = slice(2 * cc * 64, (2 * cc + 2) * 64)
                P.mm(ps[0][0][0:64, 0:64], rtc_h[lo, cc, :], shat_h[lo, cc, :],
                     reads=[rtc_b, shat_b], wbuf=ps[0][1])
                P.mm(ps[0][0][0:64, 64:128], rtc_h[hi, cc, :], shat_h[hi, cc, :],
                     reads=[rtc_b, shat_b], wbuf=ps[0][1])
                P.stt(ych_h[:, pc], ps[0][0][0:64, 0:128], 1.0, ych_h[:, pc],
                      ALU.mult, ALU.add, reads=[ps[0][1]], writes=[ych_b])
            P.act(ysq_h[:], ych_h[:], AF.Square, reads=[ych_b], writes=[ysq_b])
            P.raw("dve", lambda e, o=ym_h, i=ych_h:
                  e.reduce_sum(o[:], i[:].rearrange("p (h i) -> p h i", h=32),
                               mybir.AxisListType.X),
                  reads=[ych_b], writes=[ym_b])
            P.raw("dve", lambda e, o=yv_h, i=ysq_h:
                  e.reduce_sum(o[:], i[:].rearrange("p (h i) -> p h i", h=32),
                               mybir.AxisListType.X),
                  reads=[ysq_b], writes=[yv_b])
            P.amul(ym_h[:], ym_h[:], 1.0 / 64, reads=[ym_b], writes=[ym_b])
            P.amul(yv_h[:], yv_h[:], 1.0 / 64, reads=[yv_b], writes=[yv_b])
            P.stt(gt1_h[:], ym_h[:], -1.0, ym_h[:], ALU.mult, ALU.mult,
                  reads=[ym_b], writes=[gt1_b])
            P.tt(gt1_h[:], yv_h[:], gt1_h[:], ALU.add, reads=[yv_b, gt1_b], writes=[gt1_b])
            P.act(gt1_h[:], gt1_h[:], AF.Ln, reads=[gt1_b, epsg_b], writes=[gt1_b],
                  bias=epsg_h[:])
            P.act(gnr_h[:], gt1_h[:], AF.Exp, reads=[gt1_b], writes=[gnr_b], scale=-0.5)
            for h in range(32):
                P.ts(ych_h[:, h * 64:(h + 1) * 64], ych_h[:, h * 64:(h + 1) * 64],
                     ym_h[:, h:h + 1], ALU.subtract, reads=[ym_b, gnr_b], writes=[ych_b],
                     s2=gnr_h[:, h:h + 1], op1=ALU.mult)
            P.tt(ych_h[:], ych_h[:], garow_h[:], ALU.mult, reads=[garow_b], writes=[ych_b])
            P.tt(ych_h[:], ych_h[:], bbrow_h[:], ALU.add, reads=[bbrow_b], writes=[ych_b])
            P.tt(ych_h[:], ych_h[:], gch_h[:], ALU.mult, reads=[gch_b], writes=[ych_b])
            for cc in range(KC):
                P.transpose(pb0_h[0:128, 0:64], ych_h[:, cc * 128:(cc + 1) * 128],
                            cb["ident_b"][0:64, 0:64], reads=[ych_b, cb["ident_b/b"]],
                            wbuf=pb0_b)
                P.cp(ygf_h[:, cc, (ch % 8) * 64:(ch % 8 + 1) * 64], pb0_h[0:128, 0:64],
                     reads=[pb0_b], writes=[ygf_b])
            if ch % 8 == 7:
                tb = idx // 8
                sl = slice(tb * 512, (tb + 1) * 512)
                for oc in range(KC):
                    psh, psb = ps[2 + oc % 2]
                    for cc in range(KC):
                        P.mm(psh[:, 0:512], woB_h[:, cc, oc * 128:(oc + 1) * 128],
                             ygf_h[:, cc, :], reads=[woB_b, ygf_b], wbuf=psb,
                             start=(cc == 0), stop=(cc == KC - 1))
                    P.cp(stg_h[:, oc, :], psh[:, 0:512], reads=[psb], writes=[stg_b], eng="act")
                P.dma("sp", T["s0"].rearrange("(oc p) t -> p oc t", p=128)[:, :, sl],
                      stg_h[:], reads=[stg_b])

    xnW_h, xnW_b = P.sbuf("xnWC", [128, KC, 512], BF16)
    gw_h, gw_b = P.sbuf("gwC", [128, KC, 512], BF16)
    uw_h, uw_b = P.sbuf("uwC", [128, KC, 512], BF16)
    gsil_h, gsil_b = P.sbuf("gsilC", [128, 512], BF16)
    hh_h, hh_b = P.sbuf("hhC", [128, 512], BF16)
    habs_h, habs_b = P.sbuf("habsC", [128, 512], BF16)
    tA_h, tA_b = P.sbuf("tAC", [128, 128], BF16)
    tB_h, tB_b = P.sbuf("tBC", [128, 128], BF16)
    tC_h, tC_b = P.sbuf("tCC", [128, 128], BF16)
    tD_h, tD_b = P.sbuf("tDC", [128, 128], BF16)
    thr_h, thr_b = P.sbuf("thrC", [128, 128], BF16)
    msk_h, msk_b = P.sbuf("mskC", [128, 128], U32)
    zt_h, zt_b = P.sbuf("ztC", [128, 128], BF16)
    hs_h, hs_b = P.sbuf("hsC", [128, 512], BF16)
    hstg_flat = woB_h[:].rearrange("p a b -> p (a b)")
    hstg_b = woB_b
    def hstg_v(ic, t0, t1):
        return hstg_flat[:, ic * 512 + t0: ic * 512 + t1]
    dwn_flat = ygf_h[:].rearrange("p a b -> p (a b)")
    dwn_b = ygf_b
    def dwn_v(ic, c0, c1):
        return dwn_flat[:, ic * 128 + c0: ic * 128 + c1]
    g0_h, g0_b = P.sbuf("g0C", [128, 512], BF16)
    xc_h, xc_b = P.sbuf("xcC", [128, 512], F32)
    sc_h, sc_b = P.sbuf("scC", [128, 512], BF16)
    sin_flat = sin_h[:].rearrange("p a b c -> p (a b c)")
    sg_h = sin_flat[:, 0:512]
    sg_b = sin_b
    o0_h = sin_flat[:, 512:1024]
    o0_b = sin_b
    P.memset(zt_h[:], 0.0, writes=[zt_b])

    f0g_r = T["f0gate"].rearrange("(c p) i -> p c i", p=128)
    f0u_r = T["f0up"].rearrange("(c p) i -> p c i", p=128)
    f0d_r = T["f0down"].rearrange("(ic p) c -> p ic c", p=128)
    xn_r = T["xn"].rearrange("(c p) t -> p c t", p=128)
    s0_r = T["s0"].rearrange("(c p) t -> p c t", p=128)
    o0_r = T["out0"].rearrange("(c p) t -> p c t", p=128)

    for tb in range(4):
        sl = slice(tb * 512, (tb + 1) * 512)
        b, off = tb // 2, (tb % 2) * 512
        P.dma("sp", xnW_h[:], xn_r[:, :, sl], writes=[xnW_b])
        for ib in range(16):
            P.dma("pool", gw_h[:], f0g_r[:, :, ib * 512:(ib + 1) * 512], writes=[gw_b])
            P.dma("pool", uw_h[:], f0u_r[:, :, ib * 512:(ib + 1) * 512], writes=[uw_b])
            for tc4 in range(4):
                for cc in range(KC):
                    P.mm(ps[0][0][:, 0:512], xnW_h[:, cc, tc4 * 128:(tc4 + 1) * 128],
                         gw_h[:, cc, :], reads=[xnW_b, gw_b], wbuf=ps[0][1],
                         start=(cc == 0), stop=(cc == KC - 1))
                for cc in range(KC):
                    P.mm(ps[1][0][:, 0:512], xnW_h[:, cc, tc4 * 128:(tc4 + 1) * 128],
                         uw_h[:, cc, :], reads=[xnW_b, uw_b], wbuf=ps[1][1],
                         start=(cc == 0), stop=(cc == KC - 1))
                P.act(gsil_h[:], ps[0][0][:, 0:512], AF.Silu, reads=[ps[0][1]], writes=[gsil_b])
                P.tt(hh_h[:], ps[1][0][:, 0:512], gsil_h[:], ALU.mult,
                     reads=[ps[1][1], gsil_b], writes=[hh_b])
                P.act(habs_h[:], hh_h[:], AF.Abs, reads=[hh_b], writes=[habs_b])
                P.tt(tA_h[:], habs_h[:, 0:128], habs_h[:, 128:256], ALU.max,
                     reads=[habs_b], writes=[tA_b])
                P.tt(tB_h[:], habs_h[:, 0:128], habs_h[:, 128:256], ALU.min,
                     reads=[habs_b], writes=[tB_b])
                P.tt(tC_h[:], habs_h[:, 256:384], habs_h[:, 384:512], ALU.max,
                     reads=[habs_b], writes=[tC_b])
                P.tt(tD_h[:], habs_h[:, 256:384], habs_h[:, 384:512], ALU.min,
                     reads=[habs_b], writes=[tD_b])
                P.tt(tA_h[:], tA_h[:], tC_h[:], ALU.min, reads=[tA_b, tC_b], writes=[tA_b])
                P.tt(tB_h[:], tB_h[:], tD_h[:], ALU.max, reads=[tB_b, tD_b], writes=[tB_b])
                P.tt(thr_h[:], tA_h[:], tB_h[:], ALU.max, reads=[tA_b, tB_b], writes=[thr_b])
                for l in range(4):
                    lsl = slice(l * 128, (l + 1) * 128)
                    P.tt(msk_h[:], habs_h[:, lsl], thr_h[:], ALU.is_ge,
                         reads=[habs_b, thr_b], writes=[msk_b])
                    P.sel(hs_h[:, lsl], msk_h[:], hh_h[:, lsl], zt_h[:],
                          reads=[msk_b, hh_b, zt_b], writes=[hs_b])
                for l in range(4):
                    lsl = slice(l * 128, (l + 1) * 128)
                    P.transpose(pb0_h[0:128, 0:128], hs_h[:, lsl], cb["ident_b"][:],
                                reads=[hs_b, cb["ident_b/b"]], wbuf=pb0_b)
                    P.cp(hstg_v(ib * 4 + l, tc4 * 128, (tc4 + 1) * 128),
                         pb0_h[0:128, 0:128], reads=[pb0_b], writes=[hstg_b])
        for cc in range(KC):
            P.dma("pool", dwn_flat[:, 0:8192].rearrange("p (a b) -> p a b", a=64),
                  f0d_r[:, :, cc * 128:(cc + 1) * 128], writes=[dwn_b])
            for ic in range(64):
                P.mm(ps[2][0][:, 0:512], dwn_v(ic, 0, 128), hstg_v(ic, 0, 512),
                     reads=[dwn_b, hstg_b], wbuf=ps[2][1],
                     start=(ic == 0), stop=(ic == 63))
            P.act(g0_h[:], ps[2][0][:, 0:512], AF.Silu, reads=[ps[2][1]], writes=[g0_b])
            P.dma("sp", xc_h[:], T["x"][b, cc * 128:(cc + 1) * 128, 1 + off:1 + off + 512],
                  writes=[xc_b])
            P.dma("sp", sc_h[:], s0_r[:, cc, sl], writes=[sc_b])
            P.tt(sg_h, sc_h[:], g0_h[:], ALU.mult, reads=[sc_b, g0_b], writes=[sg_b])
            P.tt(o0_h, xc_h[:], sg_h, ALU.add, reads=[xc_b, sg_b], writes=[o0_b])
            P.dma("sp", o0_r[:, cc, sl], o0_h, reads=[o0_b])



def stage_d(P, T, cb):
    ps = [P.psum("ps%d" % i, [128, 512], F32) for i in range(6)]
    pb0_h, pb0_b = P.psum("pb0", [128, 1024], BF16)

    xt_h, xt_b = P.sbuf("xtD", [128, 512], F32)
    sq_h, sq_b = P.sbuf("sqD", [128, 512], F32)
    mrow_h, mrow_b = P.sbuf("mrowD", [1, 512], F32)
    vrow_h, vrow_b = P.sbuf("vrowD", [1, 512], F32)
    tr_h, tr_b = P.sbuf("trD", [1, 512], F32)
    mb_h, mb_b = P.sbuf("mbD", [128, 512], F32)
    rb_h, rb_b = P.sbuf("rbD", [128, 512], F32)
    xnt_h, xnt_b = P.sbuf("xntD", [128, 512], BF16)
    eps_h, eps_b = P.sbuf("epsD", [128, 1], F32)
    P.memset(eps_h[:], 1e-5, writes=[eps_b])
    o0_r = T["out0"].rearrange("(c p) t -> p c t", p=128)
    xn1_r = T["xn1"].rearrange("(c p) t -> p c t", p=128)
    for tb in range(4):
        sl = slice(tb * 512, (tb + 1) * 512)
        for cc in range(KC):
            P.dma("sp", xt_h[:], o0_r[:, cc, sl], writes=[xt_b])
            P.act(sq_h[:], xt_h[:], AF.Square, reads=[xt_b], writes=[sq_b])
            P.mm(ps[0][0][0:1, 0:512], cb["ones_f"][:, 0:1], xt_h[:],
                 reads=[xt_b, cb["ones_f/b"]], wbuf=ps[0][1], start=(cc == 0),
                 stop=(cc == KC - 1), inc=1)
            P.mm(ps[1][0][0:1, 0:512], cb["ones_f"][:, 0:1], sq_h[:],
                 reads=[sq_b, cb["ones_f/b"]], wbuf=ps[1][1], start=(cc == 0),
                 stop=(cc == KC - 1), inc=1)
        P.amul(mrow_h[0:1, :], ps[0][0][0:1, 0:512], 1.0 / CD, reads=[ps[0][1]], writes=[mrow_b])
        P.amul(vrow_h[0:1, :], ps[1][0][0:1, 0:512], 1.0 / CD, reads=[ps[1][1]], writes=[vrow_b])
        P.stt(tr_h[0:1, :], mrow_h[0:1, :], -1.0, mrow_h[0:1, :], ALU.mult, ALU.mult,
              reads=[mrow_b], writes=[tr_b])
        P.tt(vrow_h[0:1, :], vrow_h[0:1, :], tr_h[0:1, :], ALU.add,
             reads=[tr_b], writes=[vrow_b])
        P.act(vrow_h[0:1, :], vrow_h[0:1, :], AF.Ln, reads=[vrow_b, eps_b],
              writes=[vrow_b], bias=eps_h[0:1, :])
        P.act(vrow_h[0:1, :], vrow_h[0:1, :], AF.Exp, reads=[vrow_b], writes=[vrow_b],
              scale=-0.5)
        P.mm(ps[2][0][:, 0:512], cb["ones_f"][0:1, :], mrow_h[0:1, :],
             reads=[mrow_b, cb["ones_f/b"]], wbuf=ps[2][1])
        P.cp(mb_h[:], ps[2][0][:, 0:512], reads=[ps[2][1]], writes=[mb_b], eng="act")
        P.mm(ps[3][0][:, 0:512], cb["ones_f"][0:1, :], vrow_h[0:1, :],
             reads=[vrow_b, cb["ones_f/b"]], wbuf=ps[3][1])
        P.cp(rb_h[:], ps[3][0][:, 0:512], reads=[ps[3][1]], writes=[rb_b], eng="act")
        for cc in range(KC):
            P.dma("sp", xt_h[:], o0_r[:, cc, sl], writes=[xt_b])
            P.tt(sq_h[:], xt_h[:], mb_h[:], ALU.subtract, reads=[xt_b, mb_b], writes=[sq_b])
            P.tt(xnt_h[:], sq_h[:], rb_h[:], ALU.mult, reads=[sq_b, rb_b], writes=[xnt_b])
            P.dma("sp", xn1_r[:, cc, sl], xnt_h[:], reads=[xnt_b])

    P.fence()
    xnW_h, xnW_b = P.sbuf("xnWC", [128, KC, 512], BF16)
    wt_h, wt_b = P.sbuf("wtD", [128, KC, 128], BF16)
    zr = [P.arena("zrD%d" % i, [64, 8192], 8192 * i) for i in range(2)]
    dt2_h, dt2_b = P.sbuf("dt2D", [128, TPC], F32)

    regions = [("winT", 0, "Xs"), ("winT", 4096, "Rs"),
               ("wparamT", 128, "Bs"), ("wparamT", 4224, "Cs")]
    for tb in range(4):
        sl = slice(tb * 512, (tb + 1) * 512)
        P.dma("sp", xnW_h[:], xn1_r[:, :, sl], writes=[xnW_b])
        P.dma("pool", wt_h[:], T["wparamT"].rearrange("(c p) o -> p c o", p=128)[:, :, 0:128],
              writes=[wt_b])
        for cc in range(KC):
            P.mm(ps[4][0][:, 0:512], wt_h[:, cc, :], xnW_h[:, cc, :],
                 reads=[wt_b, xnW_b], wbuf=ps[4][1], start=(cc == 0), stop=(cc == KC - 1))
        P.cp(dt2_h[:, sl], ps[4][0][:, 0:512], reads=[ps[4][1]], writes=[dt2_b], eng="act")
        for wkey, o0, okey in regions:
            wr = T[wkey].rearrange("(c p) o -> p c o", p=128)
            for th in range(2):
                for hp in range(32):
                    if th == 0:
                        P.dma("pool", wt_h[:, :, hp % 4 * 0:128] if False else wt_h[:],
                              wr[:, :, o0 + hp * 128:o0 + (hp + 1) * 128], writes=[wt_b])
                    else:
                        P.dma("pool", wt_h[:], wr[:, :, o0 + hp * 128:o0 + (hp + 1) * 128],
                              writes=[wt_b])
                    psh, psb = ps[hp % 2]
                    for cc in range(KC):
                        P.mm(psh[:, 0:256], wt_h[:, cc, :],
                             xnW_h[:, cc, th * 256:(th + 1) * 256],
                             reads=[wt_b, xnW_b], wbuf=psb,
                             start=(cc == 0), stop=(cc == KC - 1))
                    for zq2 in range(2):
                        zq = th * 2 + zq2
                        z = tb * 4 + zq
                        tcs = slice(zq2 * 128, (zq2 + 1) * 128)
                        h0, h1 = 2 * hp, 2 * hp + 1
                        zh, zb = zr[zq2]
                        eng = "dve" if hp % 2 == 0 else "act"
                        P.cp(zh[:, h0:h0 + 64 * 127 + 1:64], psh[0:64, tcs],
                             reads=[psb], writes=[zb], eng=eng)
                        P.cp(zh[:, h1:h1 + 64 * 127 + 1:64], psh[64:128, tcs],
                             reads=[psb], writes=[zb], eng=eng)
                        if hp == 31:
                            P.dma("sp", T[okey][z], zh[:], reads=[zb])

    la_h, la_b = P.sbuf("laD", [128, TPC], F32)
    t1_h, t1_b = P.sbuf("t1D", [128, TPC], F32)
    one_h, one_b = P.sbuf("oneD", [128, 1], F32)
    P.memset(one_h[:], 1.0, writes=[one_b])
    P.act(t1_h[:], dt2_h[:], AF.Abs, reads=[dt2_b], writes=[t1_b])
    P.act(t1_h[:], t1_h[:], AF.Exp, reads=[t1_b], writes=[t1_b], scale=-1.0)
    P.act(t1_h[:], t1_h[:], AF.Ln, reads=[t1_b, one_b], writes=[t1_b], bias=one_h[:])
    P.raw("act", lambda e, o=la_h, i=dt2_h: e.activation(o[:], i[:], AF.Relu),
          reads=[dt2_b], writes=[la_b])
    P.tt(la_h[:], la_h[:], t1_h[:], ALU.add, reads=[t1_b], writes=[la_b])
    P.ts(la_h[:], la_h[:], -1.0, ALU.mult, reads=[la_b], writes=[la_b])

    laz_h, laz_b = P.sbuf("lazD", [128, 64], F32)
    csz_h, csz_b = P.sbuf("cszD", [128, 64], F32)
    dsf_h, dsf_b = P.sbuf("dsfD", [128, 64], F32)
    atb_h, atb_b = P.sbuf("atbD", [64, 64], F32)
    atr_h, atr_b = P.sbuf("atrD", [1, 64], F32)
    P.fence()
    Xz_h, Xz_b = P.arena("XzD", [64, 8192], 0)
    Bz_h, Bz_b = P.arena("BzD", [64, 8192], 8192)
    Cz_h, Cz_b = P.arena("CzD", [64, 8192], 16384)
    Rz_h, Rz_b = P.arena("RzD", [64, 8192], 24576)
    yyz_h, yyz_b = P.arena("yyzD", [64, 8256], 32768)
    yy2h_h, yy2h_b = P.arena("yy2hD", [64, 8256], 41088)
    csr_h, csr_b = P.sbuf("csrD", [1, 128], F32)
    Es_h, Es_b = P.sbuf("EsD", [128, 128], F32)
    Ms_h, Ms_b = P.sbuf("MsD", [128, 128], BF16)
    xtm_h, xtm_b = P.sbuf("xtmD", [128, 64], BF16)
    btm_h, btm_b = P.sbuf("btmD", [128, 64], BF16)
    c2_h, c2_b = P.sbuf("c2D", [64, 128], BF16)
    ex_h, ex_b = P.sbuf("exD", [64, 128], F32)
    rsil_h, rsil_b = P.sbuf("rsilD", [64, 128], BF16)
    Sd_h, Sd_b = P.sbuf("SdD", [64, 64], F32)
    Sb_h, Sb_b = P.sbuf("SbD", [64, 64], BF16)

    for z in range(16):
        P.dma("sp", Xz_h[:], T["Xs"][z], writes=[Xz_b])
        P.dma("sp", Bz_h[:], T["Bs"][z], writes=[Bz_b])
        P.dma("sp", Cz_h[:], T["Cs"][z], writes=[Cz_b])
        P.dma("sp", Rz_h[:], T["Rs"][z], writes=[Rz_b])
        P.cp(laz_h[0:64, :], la_h[0:64, z * 128:(z + 1) * 128:2], reads=[la_b], writes=[laz_b])
        P.cp(laz_h[64:128, :], la_h[64:128, z * 128 + 1:(z + 1) * 128:2], reads=[la_b], writes=[laz_b])
        P.mm(ps[0][0][:, 0:64], cb["tri_incl_f"][:], laz_h[:], reads=[laz_b, cb["tri_incl_f/b"]],
             wbuf=ps[0][1])
        P.cp(csz_h[:], ps[0][0][:, 0:64], reads=[ps[0][1]], writes=[csz_b])
        P.mm(ps[1][0][:, 0:64], cb["triu_strict_f"][:], laz_h[:],
             reads=[laz_b, cb["triu_strict_f/b"]], wbuf=ps[1][1])
        P.act(dsf_h[:], ps[1][0][:, 0:64], AF.Exp, reads=[ps[1][1]], writes=[dsf_b])
        P.mm(ps[2][0][0:64, 0:1], laz_h[:], cb["ones_f"][:, 0:1],
             reads=[laz_b, cb["ones_f/b"]], wbuf=ps[2][1])
        P.cp(atb_h[:, 0:1], ps[2][0][0:64, 0:1], reads=[ps[2][1]], writes=[atb_b])
        P.transpose(ps[3][0][0:1, 0:64], atb_h[:, 0:1], cb["ident_f"][0:64, 0:64],
                    reads=[atb_b, cb["ident_f/b"]], wbuf=ps[3][1])
        P.act(atr_h[:], ps[3][0][0:1, 0:64], AF.Exp, reads=[ps[3][1]], writes=[atr_b])
        P.mm(ps[3][0][0:64, 0:64], cb["ones_f"][0:1, 0:64], atr_h[0:1, :],
             reads=[atr_b, cb["ones_f/b"]], wbuf=ps[3][1])
        P.cp(atb_h[:, 0:64], ps[3][0][0:64, 0:64], reads=[ps[3][1]], writes=[atb_b])
        P.memset(Sd_h[:], 0.0, writes=[Sd_b])
        P.memset(Sb_h[:], 0.0, writes=[Sb_b])
        for c in range(64):
            jsl = slice(c * 128, (c + 1) * 128)
            P.transpose(ps[5][0][0:1, 0:128], csz_h[:, c:c + 1], cb["ident_f"][:],
                        reads=[csz_b, cb["ident_f/b"]], wbuf=ps[5][1])
            P.cp(csr_h[:], ps[5][0][0:1, 0:128], reads=[ps[5][1]], writes=[csr_b])
            P.mm(ps[1][0][:, 0:128], cb["ones_f"][0:1, :], csr_h[0:1, :],
                 reads=[csr_b, cb["ones_f/b"]], wbuf=ps[1][1])
            P.ts(Es_h[:], ps[1][0][:, 0:128], csz_h[:, c:c + 1], ALU.subtract,
                 reads=[ps[1][1], csz_b], writes=[Es_b])
            P.tt(Es_h[:], Es_h[:], cb["maskadd_f"][:], ALU.add,
                 reads=[cb["maskadd_f/b"]], writes=[Es_b])
            P.act(Es_h[:], Es_h[:], AF.Exp, reads=[Es_b], writes=[Es_b])
            P.mm(ps[0][0][:, 0:128], Bz_h[:, jsl], Cz_h[:, jsl],
                 reads=[Bz_b, Cz_b], wbuf=ps[0][1])
            P.tt(Ms_h[:], ps[0][0][:, 0:128], Es_h[:], ALU.mult,
                 reads=[ps[0][1], Es_b], writes=[Ms_b])
            P.transpose(pb0_h[0:128, 0:64], Xz_h[:, jsl], cb["ident_b"][0:64, 0:64],
                        reads=[Xz_b, cb["ident_b/b"]], wbuf=pb0_b)
            P.cp(xtm_h[:], pb0_h[0:128, 0:64], reads=[pb0_b], writes=[xtm_b])
            P.transpose(pb0_h[0:128, 64:128], Bz_h[:, jsl], cb["ident_b"][0:64, 0:64],
                        reads=[Bz_b, cb["ident_b/b"]], wbuf=pb0_b)
            P.ts(btm_h[:], pb0_h[0:128, 64:128], dsf_h[:, c:c + 1], ALU.mult,
                 reads=[pb0_b, dsf_b], writes=[btm_b])
            P.act(ex_h[0:1, :], csr_h[:], AF.Exp, reads=[csr_b], writes=[ex_b])
            P.mm(ps[4][0][0:64, 0:128], cb["ones_f"][0:1, 0:64], ex_h[0:1, :],
                 reads=[ex_b, cb["ones_f/b"]], wbuf=ps[4][1])
            P.tt(c2_h[:], Cz_h[:, jsl], ps[4][0][0:64, 0:128], ALU.mult,
                 reads=[Cz_b, ps[4][1]], writes=[c2_b])
            P.mm(ps[2][0][0:64, 0:128], xtm_h[:], Ms_h[:], reads=[xtm_b, Ms_b],
                 wbuf=ps[2][1], start=True, stop=False)
            P.mm(ps[2][0][0:64, 0:128], Sb_h[:], c2_h[:], reads=[Sb_b, c2_b],
                 wbuf=ps[2][1], start=False, stop=True)
            P.act(rsil_h[:], Rz_h[:, jsl], AF.Silu, reads=[Rz_b], writes=[rsil_b])
            P.tt(yyz_h[:, jsl], ps[2][0][0:64, 0:128], rsil_h[:], ALU.mult,
                 reads=[ps[2][1], rsil_b], writes=[yyz_b])
            P.mm(ps[3][0][0:64, 0:64], btm_h[:], xtm_h[:], reads=[btm_b, xtm_b],
                 wbuf=ps[3][1])
            P.stt(Sd_h[:], Sd_h[:], atb_h[:, c:c + 1], ps[3][0][0:64, 0:64],
                  ALU.mult, ALU.add, reads=[atb_b, ps[3][1]], writes=[Sd_b])
            P.cp(Sb_h[:], Sd_h[:], reads=[Sd_b], writes=[Sb_b])
        P.memset(yyz_h[:, 8192:8256], 0.0, writes=[yyz_b])
        for blk in range(16):
            bsl = slice(blk * 512, (blk + 1) * 512)
            P.mm(ps[0][0][64:128, 0:512], cb["ident_b"][0:64, 0:64],
                 yyz_h[:, blk * 512 + 1: (blk + 1) * 512 + 1],
                 reads=[yyz_b, cb["ident_b/b"]], wbuf=ps[0][1])
            P.cp(yy2h_h[:, bsl], ps[0][0][64:128, 0:512], reads=[ps[0][1]], writes=[yy2h_b])
        P.dma("sp", T["yy2"][z, 0:64, :], yyz_h[:], reads=[yyz_b])
        P.dma("sp", T["yy2"][z, 64:128, :], yy2h_h[:], reads=[yy2h_b])

    P.fence()
    wout_h, wout_b = P.arena("woutD", [128, KC, 2048], 0)
    yy2_h, yy2_b = P.arena("yy2D", [128, 8256], 32768)
    st4f_h, st4f_b = P.sbuf("st4fD", [128, KC, 128], F32)
    st4b_h, st4b_b = P.sbuf("st4bD", [128, KC, 128], BF16)
    s1_r = T["s1"].rearrange("(cc p) t -> p cc t", p=128)
    lg_r = T["logd"].rearrange("(cc p) t -> p cc t", p=128)
    for ih in range(2):
        P.fence()
        for q in range(4):
            P.dma("pool", wout_h[:, 4 * q:4 * (q + 1), :],
                  T["woutT"].rearrange("(ic p) c -> p ic c", p=128)[:, ih * 16 + 4 * q:ih * 16 + 4 * (q + 1), :],
                  writes=[wout_b])
        for z in range(16):
            P.dma("sp", yy2_h[:], T["yy2"][z], writes=[yy2_b])
            if ih == 1:
                P.dma("sp", st4f_h[:], lg_r[:, :, z * 128:(z + 1) * 128], writes=[st4f_b])
            for cc in range(KC):
                psh, psb = ps[cc % 2]
                for icl in range(KC):
                    ic = ih * 16 + icl
                    P.mm(psh[:, 0:128], wout_h[:, icl, cc * 128:(cc + 1) * 128],
                         yy2_h[:, 2 * ic: 2 * ic + 64 * 127 + 1: 64],
                         reads=[wout_b, yy2_b], wbuf=psb, start=(icl == 0), stop=(icl == KC - 1))
                if ih == 0:
                    P.cp(st4f_h[:, cc, :], psh[:, 0:128], reads=[psb], writes=[st4f_b], eng="act")
                else:
                    P.stt(st4b_h[:, cc, :], psh[:, 0:128], 1.0, st4f_h[:, cc, :],
                          ALU.mult, ALU.add, reads=[psb, st4f_b], writes=[st4b_b])
            if ih == 0:
                P.dma("sp", lg_r[:, :, z * 128:(z + 1) * 128], st4f_h[:], reads=[st4f_b])
            else:
                P.dma("sp", s1_r[:, :, z * 128:(z + 1) * 128], st4b_h[:], reads=[st4b_b])



def stage_e(P, T, cb):
    ps = [P.psum("ps%d" % i, [128, 512], F32) for i in range(6)]
    xnW_h, xnW_b = P.sbuf("xnWC", [128, KC, 512], BF16)
    gw_h, gw_b = P.arena("gwE", [128, KC, 512], 40960)
    uw_h, uw_b = P.arena("uwE", [128, KC, 512], 49152)
    P.fence()
    hg_h, hg_b = P.arena("hgE", [128, 64, 512], 0)
    gs_h, gs_b = P.arena("gsE", [128, 512], 57344)
    dwn_h, dwn_b = P.arena("dwnE", [128, 64, 128], 32768)
    g1_h, g1_b = P.arena("g1E", [128, 512], 57856)
    oc_hv, oc_b = P.arena("ocE", [128, 1024], 58368)
    sc_h, sc_b = P.arena("scE", [128, 512], 60416)
    dt2E_h, dt2E_b = P.sbuf("dt2D", [128, TPC], F32)
    fin_h = dt2E_h[:, 0:512]
    fin_b = dt2E_b
    oc_h = dt2E_h[:, 512:1024]
    oc_b = dt2E_b

    xn1_r = T["xn1"].rearrange("(c p) t -> p c t", p=128)
    o0_r = T["out0"].rearrange("(c p) t -> p c t", p=128)
    s1_r = T["s1"].rearrange("(c p) t -> p c t", p=128)
    g_r = T["f1gateT"].rearrange("(c p) h -> p c h", p=128)
    u_r = T["f1upT"].rearrange("(c p) h -> p c h", p=128)
    d_r = T["f1downT"].rearrange("(h p) c -> p h c", p=128)

    for tb in range(4):
        sl = slice(tb * 512, (tb + 1) * 512)
        b, off = tb // 2, (tb % 2) * 512
        P.dma("sp", xnW_h[:], xn1_r[:, :, sl], writes=[xnW_b])
        for hq in range(16):
            P.dma("pool", gw_h[:, :, 0:512].rearrange("p c (a q) -> p c a q", a=4),
                  g_r[:, :, hq * 512:(hq + 1) * 512].rearrange("p c (a q) -> p c a q", a=4),
                  writes=[gw_b])
            P.dma("pool", uw_h[:, :, 0:512].rearrange("p c (a q) -> p c a q", a=4),
                  u_r[:, :, hq * 512:(hq + 1) * 512].rearrange("p c (a q) -> p c a q", a=4),
                  writes=[uw_b])
            for a in range(4):
                hc = hq * 4 + a
                psh, psb = ps[a % 2]
                pu, pub = ps[2 + a % 2]
                for cc in range(KC):
                    P.mm(psh[:, 0:512], gw_h[:, cc, a * 128:(a + 1) * 128], xnW_h[:, cc, :],
                         reads=[gw_b, xnW_b], wbuf=psb, start=(cc == 0), stop=(cc == KC - 1))
                for cc in range(KC):
                    P.mm(pu[:, 0:512], uw_h[:, cc, a * 128:(a + 1) * 128], xnW_h[:, cc, :],
                         reads=[uw_b, xnW_b], wbuf=pub, start=(cc == 0), stop=(cc == KC - 1))
                P.act(gs_h[:], psh[:, 0:512], AF.Silu, reads=[psb], writes=[gs_b])
                P.tt(hg_h[:, hc, :], pu[:, 0:512], gs_h[:], ALU.mult,
                     reads=[pub, gs_b], writes=[hg_b])
        for dc in range(KC):
            P.dma("pool", dwn_h[:], d_r[:, :, dc * 128:(dc + 1) * 128], writes=[dwn_b])
            for hc in range(64):
                P.mm(ps[4][0][:, 0:512], dwn_h[:, hc, :], hg_h[:, hc, :],
                     reads=[dwn_b, hg_b], wbuf=ps[4][1], start=(hc == 0), stop=(hc == 63))
            P.act(g1_h[:], ps[4][0][:, 0:512], AF.Silu, reads=[ps[4][1]], writes=[g1_b])
            P.dma("sp", oc_h, o0_r[:, dc, sl], writes=[oc_b])
            P.dma("sp", sc_h[:], s1_r[:, dc, sl], writes=[sc_b])
            P.tt(fin_h, sc_h[:], g1_h[:], ALU.mult, reads=[sc_b, g1_b], writes=[fin_b])
            P.tt(fin_h, oc_h, fin_h, ALU.add, reads=[oc_b], writes=[fin_b])
            P.dma("sp", T["y"][b, dc * 128:(dc + 1) * 128, off:off + 512], fin_h,
                  reads=[fin_b])



def _build_launch1():
    nc = bass.Bass("TRN2", target_bir_lowering=False, debug=False)
    ins = ["x", "ident", "ones", "masksl2", "maacols", "tdecay", "faaaa",
           "maa_w1", "maa_w2", "dw1", "dw2", "WrT", "WkT", "WvT", "WgT"]
    T = declare(nc, ins, "ExternalInput")
    T.update(declare(nc, ["xn", "gt", "yloc", "rt", "dpref", "sloc", "ldtot"],
                     "ExternalOutput"))
    T.update(declare(nc, ["xxn", "kb", "rb", "vt", "logd"], "Internal"))
    with ExitStack() as stack:
        P = Prog(nc, stack, arena_elems=51200)
        cb = load_consts(P, T)
        stage_a(P, T, cb)
        stage_b(P, T, cb)
        P.drain_dma("sp")
        P.finalize()
    return nc, ins

def _build_launch2():
    nc = bass.Bass("TRN2", target_bir_lowering=False, debug=False)
    ins = ["x", "ident", "ones", "lngrow", "lnbrow", "cmask", "amask",
           "WoT", "f0gate", "f0up", "f0down",
           "xn", "gt", "yloc", "rt", "dpref", "slocg", "ldtotg"]
    T = {}
    for n in ins:
        spec = INPUT_SPECS.get(n) or SCRATCH_SPECS.get(n)
        T[n] = nc.dram_tensor(n, spec[0], spec[1], kind="ExternalInput").ap()
    T.update(declare(nc, ["out0"], "ExternalOutput"))
    T.update(declare(nc, ["s0"], "Internal"))
    with ExitStack() as stack:
        P = Prog(nc, stack)
        cb = load_consts(P, T)
        stage_c(P, T, cb)
        P.drain_dma("sp")
        P.finalize()
    return nc, ins

def _build_launch3():
    nc = bass.Bass("TRN2", target_bir_lowering=False, debug=False)
    ins = ["ident", "ones", "tri_incl", "triu_strict", "maskadd",
           "winT", "wparamT", "woutT", "f1gateT", "f1upT", "f1downT", "out0"]
    T = {}
    for n in ins:
        spec = INPUT_SPECS.get(n) or SCRATCH_SPECS.get(n)
        T[n] = nc.dram_tensor(n, spec[0], spec[1], kind="ExternalInput").ap()
    T.update(declare(nc, ["y"], "ExternalOutput"))
    T.update(declare(nc, ["xn1", "Xs", "Bs", "Cs", "Rs", "yy2", "s1", "logd"],
                     "Internal"))
    with ExitStack() as stack:
        P = Prog(nc, stack)
        cb = load_consts(P, T)
        stage_d(P, T, cb)
        stage_e(P, T, cb)
        P.drain_dma("sp")
        P.finalize()
    return nc, ins

_CACHE = {}

def kernel(**inputs):
    in_maps = host_prep(inputs)
    if "l1" not in _CACHE:
        _CACHE["l1"] = _build_launch1()
        _CACHE["l2"] = _build_launch2()
        _CACHE["l3"] = _build_launch3()
    nc1, ins1 = _CACHE["l1"]
    nc2, ins2 = _CACHE["l2"]
    nc3, ins3 = _CACHE["l3"]

    m1 = [{k: m[k] for k in ins1} for m in in_maps]
    r1 = run_bass_kernel_spmd(nc1, m1, core_ids=list(range(NCORES))).results

    slocg = np.stack([r["sloc"] for r in r1])
    ldtotg = np.stack([r["ldtot"] for r in r1])
    m2 = []
    for ci, m in enumerate(in_maps):
        d = {k: m[k] for k in ins2 if k in m}
        for k in ("xn", "gt", "yloc", "rt", "dpref"):
            d[k] = np.asarray(r1[ci][k])
        d["slocg"] = slocg
        d["ldtotg"] = ldtotg
        m2.append(d)
    r2 = run_bass_kernel_spmd(nc2, m2, core_ids=list(range(NCORES))).results

    m3 = []
    for ci, m in enumerate(in_maps):
        d = {k: m[k] for k in ins3 if k in m}
        d["out0"] = np.asarray(r2[ci]["out0"])
        m3.append(d)
    r3 = run_bass_kernel_spmd(nc3, m3, core_ids=list(range(NCORES))).results

    out = np.empty((2, 8192, CD), np.float32)
    for ci in range(NCORES):
        out[:, ci * TB:(ci + 1) * TB, :] = np.asarray(r3[ci]["y"]).transpose(0, 2, 1)
    return out
